# version 02 (final)
# baseline (speedup 1.0000x reference)
import numpy as np
import ml_dtypes
from contextlib import ExitStack
import concourse.bass as bass
import concourse.mybir as mybir
from concourse.bass_utils import run_bass_kernel_spmd

F32 = mybir.dt.float32
BF16 = mybir.dt.bfloat16
F16 = mybir.dt.float16
AF = mybir.ActivationFunctionType
ALU = mybir.AluOpType
AX = mybir.AxisListType

T = 4096
NTG = 8
TGS = 512
D = 2048
KC = D // 128
DFF = 8192
FFC = DFF // 128
EPS_RMS = 1.1920929e-07
EPS_LN = 1e-5

DEBUG = set()
N_STAGES = 12
S3_PIPE = True


class Prog:
    ENGS = ("sync", "tensor", "vector", "scalar", "gpsimd")

    def __init__(self):
        self.ops = {e: [] for e in self.ENGS}
        self.cnt = {}
        self.semnames = []
        self.last = {e: None for e in self.ENGS}

    def _sem(self, s):
        if s not in self.cnt:
            self.cnt[s] = 0
            self.semnames.append(s)

    def after(self, s):
        self._sem(s)
        return self.cnt[s]

    def do(self, eng, method, *args, inc=(), **kwargs):
        incs = []
        for s, k in inc:
            self._sem(s)
            self.cnt[s] += k
            incs.append((s, k))

        def fn(e, sems, args=args, kwargs=kwargs, incs=incs, method=method):
            ins = getattr(e, method)(*args, **kwargs)
            for s, k in incs:
                ins = ins.then_inc(sems[s], k)
        self.ops[eng].append(fn)
        self.last[eng] = incs
        return self.cnt[incs[-1][0]] if incs else None

    def add_inc(self, eng, s, k=1):
        self._sem(s)
        self.cnt[s] += k
        assert self.last[eng] is not None
        self.last[eng].append((s, k))
        return self.cnt[s]

    def wait(self, eng, s, val):
        if val is None or val <= 0:
            self._sem(s)
            return

        def fn(e, sems, s=s, val=val):
            e.wait_ge(sems[s], val)
        self._sem(s)
        self.ops[eng].append(fn)

    def xwait(self, eng, s):
        self.wait(eng, s, self.after(s))

    def barrier(self):
        engines = [e for e in self.ENGS if self.last[e] is not None]
        for e in engines:
            self.do(e, "nop", inc=[("BAR", 1)])
        v = self.after("BAR")
        for e in self.ENGS:
            self.wait(e, "BAR", v)


def host_prep(w):
    f32 = np.float32
    c = lambda a: np.ascontiguousarray(np.asarray(a), dtype=f32)
    prep = {}
    prep["wqkvT"] = c(np.asarray(w["L0_attn_w_qkv"]).T)
    sw = np.asarray(w["L0_attn_w_swiglu"]).T
    sw = sw.reshape(2048, 2, 64, 128).transpose(0, 2, 1, 3).reshape(2048, 16384)
    prep["wswT"] = c(sw)
    prep["woutT"] = c(np.asarray(w["L0_attn_w_out"]).T)
    prep["w0T"] = c(np.asarray(w["L0_ffn_W0"]).T)
    prep["w1T"] = c(np.asarray(w["L0_ffn_W1"]).T)
    prep["b0c"] = c(np.asarray(w["L0_ffn_bias0"]).reshape(64, 128).T)
    prep["b1c"] = c(np.asarray(w["L0_ffn_bias1"]).reshape(16, 128).T)
    rot = D // 2
    perm = np.concatenate([np.arange(0, rot, 2), np.arange(1, rot, 2),
                           np.arange(rot, D)])
    win = np.asarray(w["L1_attn_W_in"])
    winp = np.concatenate([win[:, 0:2048][:, perm], win[:, 2048:4096][:, perm],
                           win[:, 4096:6144][:, perm]], axis=1)
    prep["winP"] = c(winp)
    b_in = np.asarray(w["L1_attn_b_in"])
    binp = np.concatenate([b_in[0:2048][perm], b_in[2048:4096][perm],
                           b_in[4096:6144][perm]])
    prep["binc"] = c(binp.reshape(48, 128).T)
    prep["woutP"] = c(np.asarray(w["L1_attn_W_out"])[perm, :])
    prep["boutc"] = c(np.asarray(w["L1_attn_b_out"]).reshape(16, 128).T)
    prep["gammac"] = c(np.asarray(w["L1_attn_gamma"]).reshape(16, 128).T)
    prep["betac"] = c(np.asarray(w["L1_attn_beta"]).reshape(16, 128).T)
    prep["wffnT"] = c(np.asarray(w["L1_ffn_W"]).T)
    prep["bffnc"] = c(np.asarray(w["L1_ffn_bias"]).reshape(16, 128).T)
    prep["rw1T"] = c(np.asarray(w["L1_ffn_rw1"]).T)
    prep["rb1c"] = c(np.asarray(w["L1_ffn_rb1"]).reshape(64, 1))
    prep["rw2T"] = c(np.asarray(w["L1_ffn_rw2"]).T)
    prep["rb2c"] = c(np.asarray(w["L1_ffn_rb2"]).reshape(32, 1))
    prep["gatesc"] = c(np.asarray(w["L1_ffn_gates"]).reshape(16, 1))
    prep["aflatT"] = c(np.asarray(w["L1_ffn_A"]).reshape(256, 2048).T)
    prep["btT"] = c(np.asarray(w["L1_ffn_B"]).transpose(0, 2, 1).reshape(256, 2048))
    inv = 1.0 / 10000.0 ** (np.arange(0, rot, 2, dtype=np.float64) / rot)
    fr = np.arange(T, dtype=np.float64)[:, None] * inv[None, :]
    prep["rotcos"] = c(np.cos(fr).T)
    prep["rotsin"] = c(np.sin(fr).T)
    prep["ident"] = c(np.eye(128))
    rep0 = np.zeros((16, 128), f32)
    rep1 = np.zeros((16, 128), f32)
    for pp in range(128):
        rep0[pp // 16, pp] = 1.0
        rep1[pp // 16 + 8, pp] = 1.0
    prep["rep0"] = rep0
    prep["rep1"] = rep1
    return prep


WEIGHT_SHAPES = [
    ("wqkvT", [D, 3 * D]), ("wswT", [D, 2 * DFF]), ("woutT", [DFF, D]),
    ("w0T", [D, DFF]), ("w1T", [DFF, D]), ("b0c", [128, 64]), ("b1c", [128, 16]),
    ("winP", [D, 3 * D]), ("binc", [128, 48]), ("woutP", [D, D]),
    ("boutc", [128, 16]), ("gammac", [128, 16]), ("betac", [128, 16]),
    ("wffnT", [D, D]), ("bffnc", [128, 16]), ("rw1T", [D, 64]), ("rb1c", [64, 1]),
    ("rw2T", [64, 32]), ("rb2c", [32, 1]), ("gatesc", [16, 1]),
    ("aflatT", [D, 256]), ("btT", [256, D]),
    ("rotcos", [512, T]), ("rotsin", [512, T]),
    ("ident", [128, 128]), ("rep0", [16, 128]), ("rep1", [16, 128]),
]


def build_program():
    nc = bass.Bass("TRN2", target_bir_lowering=False, debug=False)
    p = Prog()

    def dram(name, shape, dt=F32):
        kind = "ExternalOutput" if name in DEBUG else "Internal"
        return nc.dram_tensor(name, shape, dt, kind=kind).ap()

    x_in = nc.dram_tensor("x", [T, D], F32, kind="ExternalInput").ap()
    y_out = nc.dram_tensor("y", [T, D], F32, kind="ExternalOutput").ap()
    W = {}
    for name, shape in WEIGHT_SHAPES:
        W[name] = nc.dram_tensor(name, shape, F32, kind="ExternalInput").ap()

    x0f = dram("x0f", [D, T])
    ainf = dram("ainf", [D, T], BF16)
    xn16f = dram("xn16f", [D, T], BF16)
    qf = dram("qf", [D, T])
    kf = dram("kf", [D, T])
    vf = dram("vf", [D, T])
    usgf = dram("usgf", [DFF, T], BF16)
    sf = dram("sf", [D, T])
    df = dram("df", [DFF, T], BF16)
    g0f = dram("g0f", [D, T])
    x1f = dram("x1f", [D, T])
    Xnf = dram("Xnf", [D, T], BF16)
    kpf = dram("kpf", [D, T])
    vpf = dram("vpf", [D, T])
    q1f = dram("q1f", [D, T], BF16)
    g1f = dram("g1f", [D, T])
    rowsd = dram("rowsd", [12, T])
    qkvR = dram("qkvR", [96, 128, 1024], BF16)
    winR = dram("winR", [96, 128, 1024], BF16)
    woutR = dram("woutR", [128, 128, 1024], BF16)
    w1R = dram("w1R", [128, 128, 1024], BF16)
    wffnR = dram("wffnR", [32, 128, 1024], BF16)
    woutPR = dram("woutPR", [32, 128, 1024], BF16)
    aflatR = dram("aflatR", [16, 128, 256], BF16)
    btR = dram("btR", [4, 128, 1024], BF16)

    with ExitStack() as ctx:
        sbt = lambda nm, sh, dt: ctx.enter_context(nc.sbuf_tensor(nm, sh, dt))
        pst = lambda nm, sh, dt=F32: ctx.enter_context(nc.psum_tensor(nm, sh, dt))

        RES = sbt("RES", [128, 65536], BF16)
        RESf = RES.ap().bitcast(F32)
        WS = sbt("WS", [128, 2, 1024], F32)
        WB = sbt("WB", [128, 8192], BF16)
        BC = sbt("BC", [128, 6, 512], F32)
        EV = sbt("EV", [128, 8, 512], F32)
        EB = sbt("EB", [128, 4, 512], BF16)
        RT = sbt("RT", [128, 2, 512], F32)
        ONES = sbt("ONES", [128, 512], BF16)
        ROW = [sbt(f"ROW{i}", [1, 512], F32) for i in range(6)]
        CONST = sbt("CONST", [128, 464], F32)
        CONSTB = sbt("CONSTB", [128, 392], BF16)
        CARRY = sbt("CARRY", [128, 272], F32)
        ACOL = sbt("ACOL", [128, 16], F32)
        LG = sbt("LG", [128, 544], F32)
        F16T = LG.ap().bitcast(F16)[:, 0:512]
        TK = sbt("TK", [128, 224], F32)

        MB = [pst(f"mb{i}", [128, 512]) for i in range(4)]
        SR = [pst(f"sr{i}", [1, 512]) for i in range(2)]
        TRB = pst("trb", [128, 128])
        XPS = pst("xps", [128, 512])

        ident_f = CONST[:, 0:128]
        b0c_t = CONST[:, 128:192]
        b1c_t = CONST[:, 192:208]
        binc_t = CONST[:, 208:256]
        boutc_t = CONST[:, 256:272]
        gammac_t = CONST[:, 272:288]
        betac_t = CONST[:, 288:304]
        bffnc_t = CONST[:, 304:320]
        rb1c_t = CONST[0:64, 320:321]
        rb2c_t = CONST[0:32, 321:322]
        gatesc_t = CONST[0:16, 322:323]
        ones_row_f = CONST[0:1, 324:452]
        ident_b = CONSTB[:, 0:128]
        ones_col_b = CONSTB[:, 128:129]
        rep0_b = CONSTB[0:16, 129:257]
        rep1_b = CONSTB[0:16, 257:385]

        CONST_VALS = [EPS_LN, EPS_RMS, 0.0, 1.0, -0.5, -1.0]
        for ci, cv in enumerate(CONST_VALS):
            nc.const_aps.register(F32, cv, CONST[:, 452 + ci:453 + ci])

        def row_sl(tg):
            return slice(tg * TGS, (tg + 1) * TGS)

        def dma_in(out, in_):
            return p.do("sync", "dma_start", out=out, in_=in_, inc=[("LD", 16)])

        def dma_out(out, in_):
            return p.do("scalar", "dma_start", out=out, in_=in_, inc=[("WR", 16)])

        rowpe = {}

        def load_row(ri, rr, tg):
            p.wait("sync", "PE", rowpe.get(ri, 0))
            return dma_in(ROW[ri][0:1, :], rowsd[rr:rr + 1, row_sl(tg)])

        def bcast(bc_slot, rowbuf, wait=None):
            if wait is not None:
                p.wait("tensor", wait[0], wait[1])
            p.do("tensor", "matmul", out=XPS[:, :], lhsT=ones_row_f,
                 rhs=ROW[rowbuf][0:1, :], start=True, stop=True, inc=[("PE", 1)])
            rowpe[rowbuf] = p.after("PE")
            p.wait("vector", "PE", p.after("PE"))
            p.do("vector", "tensor_copy", out=BC[:, bc_slot, :], in_=XPS[:, :],
                 inc=[("VE", 1)])
            v = p.after("VE")
            p.wait("tensor", "VE", v)
            return v

        def rstd_row(src_ri, dst_ri, eps, scale_mean):
            p.do("vector", "tensor_scalar", out=ROW[dst_ri][0:1, :],
                 in0=ROW[src_ri][0:1, :], scalar1=scale_mean, scalar2=None,
                 op0=ALU.mult, inc=[("VE", 1)])
            p.wait("scalar", "VE", p.after("VE"))
            p.do("scalar", "activation", out=ROW[dst_ri][0:1, :],
                 in_=ROW[dst_ri][0:1, :], func=AF.Ln, bias=eps, scale=1.0)
            p.do("scalar", "activation", out=ROW[dst_ri][0:1, :],
                 in_=ROW[dst_ri][0:1, :], func=AF.Exp, scale=-0.5, inc=[("AC", 1)])
            return p.after("AC")

        ws_hist = []

        def load_w_2048_section(wT, col0, res_off=0, ncols=2048):
            for kg in range(KC):
                nh = max(1, ncols // 1024)
                for half in range(nh):
                    wcols = min(1024, ncols)
                    ws = len(ws_hist) % 2
                    if len(ws_hist) >= 2:
                        p.wait("sync", "PO", ws_hist[-2])
                    src = wT[kg * 128:(kg + 1) * 128,
                             col0 + half * 1024: col0 + half * 1024 + wcols]
                    ld = dma_in(WS[:, ws, 0:wcols], src)
                    p.wait("gpsimd", "LD", ld)
                    dst0 = res_off + kg * ncols + half * 1024
                    ws_hist.append(p.do("gpsimd", "tensor_copy",
                                        out=RES[:, dst0:dst0 + wcols],
                                        in_=WS[:, ws, 0:wcols], inc=[("PO", 1)]))
            return ws_hist[-1]

        class WRing:
            def __init__(self):
                self.seq = 0
                self.po = {}
                self.reader = {}

            def load_block(self, wT, col0, nrows):
                npieces = nrows // 1024
                chunks = []
                first = self.seq
                for pc in range(npieces):
                    slot = self.seq % 8
                    ws = len(ws_hist) % 2
                    if len(ws_hist) >= 2:
                        p.wait("sync", "PO", ws_hist[-2])
                    if self.seq >= 8 and (self.seq - 8) in self.reader:
                        rsem, rval = self.reader[self.seq - 8]
                        p.wait("sync", rsem, rval)
                    src = wT[pc * 1024:(pc + 1) * 1024, col0:col0 + 128]
                    src = src.rearrange("(c p) f -> p c f", p=128)
                    ld = dma_in(WS[:, ws, :].rearrange("p (c f) -> p c f", f=128),
                                src)
                    p.wait("gpsimd", "LD", ld)
                    self.po[self.seq] = p.do(
                        "gpsimd", "tensor_copy",
                        out=WB[:, slot * 1024:(slot + 1) * 1024],
                        in_=WS[:, ws, :], inc=[("PO", 1)])
                    ws_hist.append(self.po[self.seq])
                    for k8 in range(8):
                        chunks.append(WB[:, slot * 1024 + k8 * 128:
                                         slot * 1024 + (k8 + 1) * 128])
                    self.seq += 1
                return chunks, self.po[self.seq - 1], first

            def mark(self, piece, pe_target, sem="PE"):
                self.reader[piece] = (sem, pe_target)

            def load_block_bf(self, R, og):
                chunks = []
                first = self.seq
                for pc in range(8):
                    slot = self.seq % 8
                    if self.seq >= 8 and (self.seq - 8) in self.reader:
                        rsem, rval = self.reader[self.seq - 8]
                        p.wait("sync", rsem, rval)
                    self.po[self.seq] = dma_in(
                        WB[:, slot * 1024:(slot + 1) * 1024], R[og * 8 + pc])
                    for k8 in range(8):
                        chunks.append(WB[:, slot * 1024 + k8 * 128:
                                         slot * 1024 + (k8 + 1) * 128])
                    self.seq += 1
                return chunks, self.po[self.seq - 1], first

        pow_mark = {}

        def prew():
            WBf = WB.ap().bitcast(F32)
            jobs = []
            def piece_2048(wT, sec_off):
                def f(i):
                    kg, half = i // 2, i % 2
                    return wT[kg * 128:(kg + 1) * 128,
                              sec_off + half * 1024: sec_off + (half + 1) * 1024]
                return f
            def piece_col(wT):
                def f(i):
                    og, pc = i // 8, i % 8
                    src = wT[pc * 1024:(pc + 1) * 1024, og * 128:(og + 1) * 128]
                    return src.rearrange("(c p) f -> p c f", p=128)
                return f
            for sec in range(3):
                jobs.append((f"qkv{sec}", piece_2048(W["wqkvT"], sec * 2048),
                             32, qkvR, 1024, sec * 32))
            for sec in range(3):
                jobs.append((f"win{sec}", piece_2048(W["winP"], sec * 2048),
                             32, winR, 1024, sec * 32))
            jobs.append(("wout", piece_col(W["woutT"]), 128, woutR, 1024, 0))
            jobs.append(("w1", piece_col(W["w1T"]), 128, w1R, 1024, 0))
            jobs.append(("wffn", piece_2048(W["wffnT"], 0), 32, wffnR, 1024, 0))
            jobs.append(("woutP", piece_2048(W["woutP"], 0), 32, woutPR, 1024, 0))
            def piece_af(i):
                return W["aflatT"][i * 128:(i + 1) * 128, 0:256]
            jobs.append(("aflat", piece_af, 16, aflatR, 256, 0))
            def piece_bt(i):
                kg, half = i // 2, i % 2
                return W["btT"][kg * 128:(kg + 1) * 128,
                                half * 1024:(half + 1) * 1024]
            jobs.append(("bt", piece_bt, 4, btR, 1024, 0))
            seq = 0
            hist = {}
            for name, srcf, npieces, dstR, cols, dst0 in jobs:
                for i in range(npieces):
                    sl = seq % 2
                    if seq >= 2:
                        p.wait("gpsimd", "POW", hist[seq - 2])
                    p.do("gpsimd", "dma_start",
                         out=WBf[:, sl * 1024:sl * 1024 + cols],
                         in_=srcf(i), inc=[("POW", 16)])
                    p.wait("gpsimd", "POW", p.after("POW"))
                    p.do("gpsimd", "tensor_copy",
                         out=WB[:, 4096 + sl * 1024:4096 + sl * 1024 + cols],
                         in_=WBf[:, sl * 1024:sl * 1024 + cols])
                    p.do("gpsimd", "dma_start", out=dstR[dst0 + i][:, 0:cols],
                         in_=WB[:, 4096 + sl * 1024:4096 + sl * 1024 + cols],
                         inc=[("POW", 16)])
                    hist[seq] = p.after("POW")
                    seq += 1
                pow_mark[name] = p.after("POW")

        def load_sec_bf(R, base_piece, res_off=0, cols=1024, npieces=32):
            t = None
            for i in range(npieces):
                t = dma_in(RES[:, res_off + i * cols: res_off + (i + 1) * cols],
                           R[base_piece + i][:, 0:cols])
            return t

        dma_in(CONST[:, 0:128], W["ident"])
        dma_in(b0c_t, W["b0c"])
        dma_in(b1c_t, W["b1c"])
        dma_in(binc_t, W["binc"])
        dma_in(boutc_t, W["boutc"])
        dma_in(gammac_t, W["gammac"])
        dma_in(betac_t, W["betac"])
        dma_in(bffnc_t, W["bffnc"])
        dma_in(rb1c_t, W["rb1c"])
        dma_in(rb2c_t, W["rb2c"])
        dma_in(gatesc_t, W["gatesc"])
        dma_in(LG[0:16, 0:128], W["rep0"])
        dma_in(LG[0:16, 128:256], W["rep1"])
        p.xwait("gpsimd", "LD")
        p.do("gpsimd", "memset", ones_row_f, 1.0)
        p.do("gpsimd", "memset", ones_col_b, 1.0)
        p.do("gpsimd", "memset", ONES[:, :], 1.0)
        for ci, cv in enumerate(CONST_VALS):
            p.do("gpsimd", "memset", CONST[:, 452 + ci:453 + ci], cv)
        p.do("gpsimd", "tensor_copy", out=ident_b, in_=ident_f)
        p.do("gpsimd", "tensor_copy", out=rep0_b, in_=LG[0:16, 0:128])
        p.do("gpsimd", "tensor_copy", out=rep1_b, in_=LG[0:16, 128:256],
             inc=[("PO", 1)])
        p.barrier()

        if N_STAGES > 0:
            prew()

        XT = [RESf[:, 0:2048], RESf[:, 2048:4096]]
        SLAB = [RESf[:, 4096 + k * 512: 4096 + (k + 1) * 512] for k in range(KC)]
        SQS = RESf[:, 12800:14848]

        def s0():
            xt_guard = {}
            for tg in range(NTG):
                sumcol = EV[:, 6, 0:4]
                sqcol = EV[:, 7, 0:4]
                for tt in range(4):
                    gt = tg * 4 + tt
                    xb = XT[gt % 2]
                    if gt >= 2:
                        gpe, gve = xt_guard[gt - 2]
                        p.wait("sync", "PE", gpe)
                        p.wait("sync", "VE", gve)
                    ld = dma_in(xb, x_in[gt * 128:(gt + 1) * 128, :])
                    p.wait("vector", "LD", ld)
                    p.do("vector", "reduce_sum", out=sumcol[:, tt:tt + 1], in_=xb,
                         axis=AX.X, inc=[("VE", 1)])
                    p.wait("scalar", "LD", ld)
                    p.wait("scalar", "VE", sqs_ve.get(gt - 1, 0))
                    p.do("scalar", "square", out=SQS, in_=xb, inc=[("AC", 1)])
                    p.xwait("vector", "AC")
                    p.do("vector", "reduce_sum", out=sqcol[:, tt:tt + 1], in_=SQS,
                         axis=AX.X, inc=[("VE", 1)])
                    sqs_ve[gt] = p.after("VE")
                    p.wait("tensor", "LD", ld)
                    for kg in range(KC):
                        p.do("tensor", "transpose", out=TRB[:, :],
                             in_=xb[:, kg * 128:(kg + 1) * 128], identity=ident_f,
                             inc=[("PE", 1)])
                        p.xwait("vector", "PE")
                        p.do("vector", "tensor_copy",
                             out=SLAB[kg][:, tt * 128:(tt + 1) * 128], in_=TRB[:, :],
                             inc=[("VE", 1)])
                        p.xwait("tensor", "VE")
                    xt_guard[gt] = (p.after("PE"), p.after("VE"))
                p.xwait("tensor", "VE")
                for rowbuf, col in ((0, sumcol), (1, sqcol)):
                    p.wait("vector", "PE", rowpe.get(rowbuf, 0))
                    for tt in range(4):
                        p.do("tensor", "transpose", out=TRB[0:1, 0:128],
                             in_=col[:, tt:tt + 1], identity=ident_f, inc=[("PE", 1)])
                        p.xwait("vector", "PE")
                        p.do("vector", "tensor_copy",
                             out=ROW[rowbuf][0:1, tt * 128:(tt + 1) * 128],
                             in_=TRB[0:1, 0:128], inc=[("VE", 1)])
                        p.xwait("tensor", "VE")
                Rv = lambda i: ROW[i][0:1, :]
                p.wait("vector", "PE", max([rowpe.get(i, 0) for i in range(6)]))
                p.do("vector", "tensor_scalar", out=Rv(2), in0=Rv(0),
                     scalar1=1.0 / D, scalar2=None, op0=ALU.mult)
                p.do("vector", "tensor_scalar", out=Rv(3), in0=Rv(1),
                     scalar1=1.0 / D, scalar2=None, op0=ALU.mult)
                p.do("vector", "tensor_tensor", out=Rv(4), in0=Rv(2), in1=Rv(2),
                     op=ALU.mult)
                p.do("vector", "tensor_tensor", out=Rv(4), in0=Rv(3), in1=Rv(4),
                     op=ALU.subtract)
                rstd_row(4, 5, EPS_LN, 1.0)
                p.xwait("vector", "AC")
                p.do("vector", "tensor_tensor", out=Rv(1), in0=Rv(2), in1=Rv(5),
                     op=ALU.mult)
                p.do("vector", "tensor_scalar", out=Rv(1), in0=Rv(1), scalar1=-1.0,
                     scalar2=None, op0=ALU.mult)
                p.do("vector", "tensor_scalar", out=Rv(3), in0=Rv(4),
                     scalar1=EPS_LN, scalar2=None, op0=ALU.add)
                p.do("vector", "reciprocal", out=Rv(3), in_=Rv(3))
                p.do("vector", "tensor_tensor", out=Rv(3), in0=Rv(4), in1=Rv(3),
                     op=ALU.mult)
                rstd_row(3, 3, EPS_RMS, 1.0)
                p.xwait("vector", "AC")
                p.do("vector", "tensor_tensor", out=Rv(0), in0=Rv(5), in1=Rv(3),
                     op=ALU.mult)
                p.do("vector", "tensor_tensor", out=Rv(2), in0=Rv(1), in1=Rv(3),
                     op=ALU.mult, inc=[("VE", 1)])
                vr = p.after("VE")
                p.wait("scalar", "VE", vr)
                dma_out(rowsd[0:1, row_sl(tg)], Rv(0))
                dma_out(rowsd[1:2, row_sl(tg)], Rv(2))
                dma_out(rowsd[2:3, row_sl(tg)], Rv(5))
                dma_out(rowsd[3:4, row_sl(tg)], Rv(1))
                bcast(0, 0, wait=("VE", vr))
                bcast(1, 2)
                bcast(2, 5)
                bcast(3, 1)
                for kg in range(KC):
                    dma_out(x0f[kg * 128:(kg + 1) * 128, row_sl(tg)], SLAB[kg])
                    if kg >= 2:
                        p.wait("vector", "WR", s0_wr.get((tg, kg - 2), 0))
                    p.do("vector", "tensor_tensor", out=EV[:, kg % 4, :],
                         in0=SLAB[kg], in1=BC[:, 0, :], op=ALU.mult)
                    p.do("vector", "tensor_tensor", out=EB[:, kg % 2, :],
                         in0=EV[:, kg % 4, :], in1=BC[:, 1, :], op=ALU.add,
                         inc=[("VE", 1)])
                    p.xwait("scalar", "VE")
                    dma_out(ainf[kg * 128:(kg + 1) * 128, row_sl(tg)],
                            EB[:, kg % 2, :])
                    p.do("vector", "tensor_tensor", out=EV[:, kg % 4, :],
                         in0=SLAB[kg], in1=BC[:, 2, :], op=ALU.mult)
                    p.do("vector", "tensor_tensor", out=F16T[:, :],
                         in0=EV[:, kg % 4, :], in1=BC[:, 3, :], op=ALU.add)
                    p.wait("vector", "WR", s0_wr2.get((tg, kg - 2), 0))
                    p.do("vector", "tensor_copy", out=EB[:, 2 + kg % 2, :],
                         in_=F16T[:, :], inc=[("VE", 1)])
                    p.xwait("scalar", "VE")
                    dma_out(xn16f[kg * 128:(kg + 1) * 128, row_sl(tg)],
                            EB[:, 2 + kg % 2, :])
                    s0_wr[(tg, kg)] = p.after("WR")
                    s0_wr2[(tg, kg)] = p.after("WR")
                p.xwait("vector", "WR")
            p.barrier()

        s0_wr = {}
        s0_wr2 = {}
        s0_po = {}
        sqs_ve = {}
        if N_STAGES > 0:
            s0()

        AIN = [[RES[:, 32768 + b * 8192 + kg * 512: 32768 + b * 8192 + (kg + 1) * 512]
                for kg in range(KC)] for b in range(2)]

        def wres(kg, og, off=0):
            base = off + kg * 2048 + og * 128
            return RES[:, base:base + 128]

        def s2_section(sec, out_dram, stat_row_idx):
            p.barrier()
            p.wait("sync", "POW", pow_mark[f"qkv{sec}"])
            ldw = load_sec_bf(qkvR, sec * 32)
            p.wait("tensor", "LD", ldw)
            ain_ld = {}
            blk_pe = {}
            evac_ve = {}
            sq_ps = {}
            for tg in range(NTG + 1):
                if tg < NTG:
                    b = tg % 2
                    if tg >= 2:
                        p.wait("sync", "PE", blk_pe[tg - 2])
                    t = None
                    for kg in range(KC):
                        t = dma_in(AIN[b][kg],
                                   ainf[kg * 128:(kg + 1) * 128, row_sl(tg)])
                    ain_ld[tg] = t
                if tg == 0:
                    continue
                ctg = tg - 1
                b = ctg % 2
                p.wait("tensor", "LD", ain_ld[ctg])
                for og in range(16):
                    bank = MB[og % 2]
                    if og >= 2:
                        p.wait("tensor", "VE", evac_ve[og - 2])
                        if stat_row_idx is not None:
                            p.wait("tensor", "AC", sq_ps[og - 2])
                    for kg in range(KC):
                        p.do("tensor", "matmul", out=bank[:, :], lhsT=wres(kg, og),
                             rhs=AIN[b][kg], start=(kg == 0), stop=(kg == KC - 1),
                             inc=([("PE", 1)] if kg == KC - 1 else []))
                    pe = p.after("PE")
                    p.wait("vector", "PE", pe)
                    p.wait("vector", "WR", s2_wr.get(og - 4, 0))
                    p.do("vector", "tensor_copy", out=EV[:, og % 4, :],
                         in_=bank[:, :], inc=[("VE", 1)])
                    evac_ve[og] = p.after("VE")
                    p.wait("scalar", "VE", evac_ve[og])
                    dma_out(out_dram[og * 128:(og + 1) * 128, row_sl(ctg)],
                            EV[:, og % 4, :])
                    s2_wr[og] = p.after("WR")
                    if stat_row_idx is not None:
                        p.wait("scalar", "PE", pe)
                        p.wait("scalar", "PS", s2_eb.get(og - 2, 0))
                        p.do("scalar", "square", out=EB[:, og % 2, :],
                             in_=bank[:, :], inc=[("AC", 1)])
                        sq_ps[og] = p.after("AC")
                        p.wait("tensor", "AC", sq_ps[og])
                        p.do("tensor", "matmul", out=SR[0][0:1, :],
                             lhsT=ones_col_b, rhs=EB[:, og % 2, :],
                             start=(og == 0), stop=(og == 15),
                             skip_group_check=True, inc=[("PS", 1)])
                        s2_eb[og] = p.after("PS")
                blk_pe[ctg] = p.after("PE")
                if stat_row_idx is not None:
                    p.xwait("vector", "PS")
                    p.do("vector", "tensor_copy", out=ROW[2][0:1, :],
                         in_=SR[0][0:1, :], inc=[("VE", 1)])
                    p.xwait("tensor", "VE")
                    rstd_row(2, 2, EPS_RMS, 1.0 / D)
                    p.xwait("scalar", "AC")
                    dma_out(rowsd[stat_row_idx:stat_row_idx + 1, row_sl(ctg)],
                            ROW[2][0:1, :])
            p.barrier()

        s2_eb = {}
        s2_wr = {}
        if N_STAGES > 1:
            s2_section(0, qf, 4)
            s2_eb = {}
            s2_section(1, kf, 5)
            s2_eb = {}
            s2_section(2, vf, None)

        YRES = [RES[:, kg * 4096:(kg + 1) * 4096] for kg in range(KC)]

        def s3():
            p.wait("sync", "POW", pow_mark["bt"])
            WBF = WB.ap().bitcast(F32)
            EVF = EV.ap().rearrange("p a b -> p (a b)")
            EBF = EB.ap().bitcast(F32)
            st = {}
            for kg in range(KC):
                ch = slice(kg * 128, (kg + 1) * 128)
                p.wait("sync", "VE", st.get(("rat", kg - 1), 0))
                lk = dma_in(WBF[:, :], kf[ch, :])
                p.wait("sync", "VE", st.get(("evf", kg - 1), 0))
                lv = dma_in(EVF[:, :], vf[ch, :])
                for tg in range(NTG):
                    lr = load_row(0, 5, tg)
                    bcast(4 + tg % 2, 0, wait=("LD", lr))
                    bcv = p.after("VE")
                    p.wait("vector", "LD", lk)
                    p.wait("vector", "VE", bcv)
                    p.do("vector", "tensor_tensor", out=WBF[:, row_sl(tg)],
                         in0=WBF[:, row_sl(tg)], in1=BC[:, 4 + tg % 2, :],
                         op=ALU.mult, inc=([("VE", 1)] if tg == NTG - 1 else []))
                p.xwait("scalar", "VE")
                p.wait("scalar", "AC", st.get(("exp", kg - 1), 0))
                p.do("scalar", "activation", out=WBF[:, :], in_=WBF[:, :],
                     func=AF.Exp, inc=[("AC", 1)])
                st[("exp", kg)] = p.after("AC")
                p.xwait("gpsimd", "AC")
                p.wait("gpsimd", "LD", lv)
                p.do("gpsimd", "tensor_tensor", out=EVF[:, :], in0=EVF[:, :],
                     in1=WBF[:, :], op=ALU.mult, inc=[("PO", 1)])
                st[("pool", kg)] = p.after("PO")
                p.xwait("vector", "PO")
                p.do("vector", "tensor_tensor_scan", out=EVF[:, :],
                     data0=ONES[:, 0:1].broadcast_to([128, 4096]),
                     data1=EVF[:, :], initial=0.0, op0=ALU.mult, op1=ALU.add)
                p.do("vector", "tensor_tensor_scan", out=WBF[:, :],
                     data0=ONES[:, 0:1].broadcast_to([128, 4096]),
                     data1=WBF[:, :], initial=0.0, op0=ALU.mult, op1=ALU.add)
                p.do("vector", "tensor_scalar", out=WBF[:, :], in0=WBF[:, :],
                     scalar1=1e-6, scalar2=None, op0=ALU.add)
                p.do("vector", "reciprocal", out=WBF[:, :], in_=WBF[:, :])
                p.do("vector", "tensor_tensor", out=EVF[:, :], in0=EVF[:, :],
                     in1=WBF[:, :], op=ALU.mult, inc=[("VE", 1)])
                st[("rat", kg)] = p.after("VE")
                for tg in range(NTG):
                    sl = row_sl(tg)
                    par = tg % 2
                    eq = EBF[:, 2 * par:2 * par + 2, :].rearrange(
                        "p a b -> p (a b)")
                    lr = load_row(1, 4, tg)
                    bcast(par, 1, wait=("LD", lr))
                    bcv = p.after("VE")
                    p.wait("sync", "VE", st.get(("yve", (kg, tg - 2)), 0))
                    lq = dma_in(eq, qf[ch, sl])
                    p.wait("vector", "LD", lq)
                    p.wait("vector", "VE", bcv)
                    p.wait("vector", "AC", st.get(("qexp", (kg, tg - 2)), 0))
                    p.do("vector", "tensor_tensor", out=eq, in0=eq,
                         in1=BC[:, par, :], op=ALU.mult, inc=[("VE", 1)])
                    p.xwait("scalar", "VE")
                    p.do("scalar", "activation", out=eq, in_=eq, func=AF.Exp,
                         scale=-1.0, inc=[("AC", 1)])
                    st[("qexp", (kg, tg))] = p.after("AC")
                    p.xwait("vector", "AC")
                    p.do("vector", "tensor_scalar", out=eq, in0=eq, scalar1=1.0,
                         scalar2=None, op0=ALU.add)
                    p.do("vector", "reciprocal", out=eq, in_=eq)
                    p.do("vector", "tensor_tensor",
                         out=YRES[kg][:, sl], in0=eq, in1=EVF[:, sl],
                         op=ALU.mult, inc=[("VE", 1)])
                    st[("yve", (kg, tg))] = p.after("VE")
                st[("evf", kg)] = p.after("VE")
            p.barrier()

        if N_STAGES > 2:
            s3()
            if "ydbg" in DEBUG:
                ydbg = dram("ydbg", [D, T], BF16)
                for kg in range(KC):
                    dma_out(ydbg[kg * 128:(kg + 1) * 128, :], YRES[kg])
                p.barrier()

        def s4_real():
            ring = WRing()

            def load_pair(ogp):
                cu, pou, fu = ring.load_block(W["wswT"], ogp * 256, 2048)
                cg, pog, fg = ring.load_block(W["wswT"], ogp * 256 + 128, 2048)
                return (cu, cg, pog, fu)

            blocks = {0: load_pair(0)}
            for ogp in range(64):
                if ogp + 1 < 64:
                    blocks[ogp + 1] = load_pair(ogp + 1)
                cu, cg, po_t, first = blocks.pop(ogp)
                p.wait("tensor", "PO", po_t)
                for tg in range(NTG):
                    gi = ogp * 8 + tg
                    bu = MB[tg % 2]
                    bg = MB[2 + tg % 2]
                    p.wait("tensor", "VE", s4_ve.get(gi - 2, 0))
                    p.wait("tensor", "AC", s4_sil.get(gi - 2, 0))
                    for kg in range(KC):
                        p.do("tensor", "matmul", out=bu[:, :], lhsT=cu[kg],
                             rhs=YRES[kg][:, row_sl(tg)],
                             start=(kg == 0), stop=(kg == KC - 1))
                    for kg in range(KC):
                        p.do("tensor", "matmul", out=bg[:, :], lhsT=cg[kg],
                             rhs=YRES[kg][:, row_sl(tg)],
                             start=(kg == 0), stop=(kg == KC - 1),
                             inc=([("PE", 1)] if kg == KC - 1 else []))
                    pe = p.after("PE")
                    p.wait("scalar", "PE", pe)
                    p.wait("scalar", "VE", s4_ve.get(gi - 2, 0))
                    p.do("scalar", "activation", out=EV[:, 4 + tg % 2, :],
                         in_=bg[:, :], func=AF.Silu, inc=[("AC", 1)])
                    s4_sil[gi] = p.after("AC")
                    p.xwait("vector", "AC")
                    p.wait("vector", "WR", s4_wr.get(gi - 2, 0))
                    p.do("vector", "tensor_tensor", out=EB[:, tg % 2, :],
                         in0=bu[:, :], in1=EV[:, 4 + tg % 2, :], op=ALU.mult,
                         inc=[("VE", 1)])
                    s4_ve[gi] = p.after("VE")
                    p.wait("scalar", "VE", s4_ve[gi])
                    dma_out(usgf[ogp * 128:(ogp + 1) * 128, row_sl(tg)],
                            EB[:, tg % 2, :])
                    s4_wr[gi] = p.after("WR")
                pe_done = p.after("PE")
                for pc in range(first, first + 4):
                    ring.mark(pc, pe_done)
            p.barrier()

        s4_ve = {}
        s4_sil = {}
        s4_wr = {}
        if N_STAGES > 3:
            s4_real()

        UBLK = [RES[:, c * 1024:(c + 1) * 1024] for c in range(FFC)]

        def s5_s7_common(in_dram, w_dram, epilogue, hoist_rows):
            ring = WRing()
            st = {}
            p.wait("sync", "POW", pow_mark["w1"])
            for tb in range(4):
                p.barrier()
                lds = None
                for c in range(FFC):
                    lds = dma_in(UBLK[c],
                                 in_dram[c * 128:(c + 1) * 128,
                                         tb * 1024:(tb + 1) * 1024])
                if hoist_rows:
                    for th in range(2):
                        tg = tb * 2 + th
                        load_row(0, 2, tg)
                        l2 = load_row(1, 3, tg)
                        bcast(2 * th, 0, wait=("LD", l2))
                        bcast(2 * th + 1, 1)
                p.wait("tensor", "LD", lds)
                for og in range(16):
                    chunks, po_t, first = ring.load_block_bf(w_dram, og)
                    p.wait("tensor", "LD", po_t)
                    for th in range(2):
                        tg = tb * 2 + th
                        bank = MB[(og * 2 + th) % 2]
                        gi = og * 2 + th
                        p.wait("tensor", "EP", st.get(("ep", gi - 2), 0))
                        for c in range(FFC):
                            if c == FFC - 1:
                                incs = [("PE", 1)]
                            elif th == 1 and (c % 8) == 7:
                                incs = [("PP", 1)]
                            else:
                                incs = []
                            p.do("tensor", "matmul", out=bank[:, :],
                                 lhsT=chunks[c],
                                 rhs=UBLK[c][:, th * 512:(th + 1) * 512],
                                 start=(c == 0), stop=(c == FFC - 1), inc=incs)
                            if th == 1 and (c % 8) == 7:
                                if c == FFC - 1:
                                    ring.mark(first + c // 8, p.after("PE") + 0,
                                              sem="PE")
                                else:
                                    ring.mark(first + c // 8, p.after("PP"),
                                              sem="PP")
                        pe = p.after("PE")
                        epilogue(og, tg, th, bank, pe, st)
            p.barrier()

        def s5_epilogue(og, tg, th, bank, pe, st):
            gi = og * 2 + th
            evc = EV[:, gi % 2, :]
            ex0 = EV[:, 2 + gi % 2, :]
            exn = EV[:, 4 + gi % 2, :]
            p.wait("scalar", "PE", pe)
            p.wait("scalar", "VE", st.get(("sve", gi - 2), 0))
            p.wait("scalar", "WR", st.get(("swr", gi - 2), 0))
            p.do("scalar", "copy", out=evc, in_=bank[:, :], inc=[("AC", 1)])
            p.do("scalar", "nop", inc=[("EP", 1)])
            st[("ep", gi)] = p.after("EP")
            ac = p.after("AC")
            p.wait("sync", "VE", st.get(("sve", gi - 2), 0))
            lx = dma_in(ex0, x0f[og * 128:(og + 1) * 128, row_sl(tg)])
            p.wait("vector", "LD", lx)
            p.do("vector", "tensor_tensor", out=exn, in0=ex0,
                 in1=BC[:, 2 * th, :], op=ALU.mult)
            p.do("vector", "tensor_tensor", out=exn, in0=exn,
                 in1=BC[:, 2 * th + 1, :], op=ALU.add)
            p.wait("vector", "AC", ac)
            p.do("vector", "tensor_tensor", out=evc, in0=evc, in1=exn,
                 op=ALU.add, inc=[("VE", 1)])
            st[("sve", gi)] = p.after("VE")
            p.wait("scalar", "VE", st[("sve", gi)])
            dma_out(sf[og * 128:(og + 1) * 128, row_sl(tg)], evc)
            st[("swr", gi)] = p.after("WR")

        def s7_epilogue(og, tg, th, bank, pe, st):
            gi = og * 2 + th
            evc = EV[:, gi % 2, :]
            evo = EV[:, 2 + gi % 2, :]
            p.wait("scalar", "PE", pe)
            p.wait("scalar", "VE", st.get(("sve", gi - 2), 0))
            p.do("scalar", "copy", out=evc, in_=bank[:, :], inc=[("AC", 1)])
            p.do("scalar", "nop", inc=[("EP", 1)])
            st[("ep", gi)] = p.after("EP")
            ac = p.after("AC")
            p.wait("vector", "AC", ac)
            p.wait("vector", "AC", st.get(("sil", gi - 1), 0))
            p.do("vector", "tensor_scalar", out=F16T[:, :], in0=evc,
                 scalar1=b1c_t[:, og:og + 1], scalar2=None, op0=ALU.add,
                 inc=[("VE", 1)])
            st[("sve", gi)] = p.after("VE")
            p.xwait("scalar", "VE")
            p.wait("scalar", "WR", st.get(("swr", gi - 2), 0))
            p.do("scalar", "activation", out=evo, in_=F16T[:, :],
                 func=AF.Silu, inc=[("AC", 1)])
            st[("sil", gi)] = p.after("AC")
            p.xwait("scalar", "AC")
            dma_out(g0f[og * 128:(og + 1) * 128, row_sl(tg)], evo)
            st[("swr", gi)] = p.after("WR")

        ep_guard = {}
        s5_wr = {}
        s7_g = {}
        s7_wr = {}
        if N_STAGES > 4:
            s5_s7_common(usgf, woutR, s5_epilogue, hoist_rows=True)

        XN16 = [RES[:, kg * 4096:(kg + 1) * 4096] for kg in range(KC)]

        def s6():
            p.barrier()
            lds = None
            for kg in range(KC):
                lds = dma_in(XN16[kg], xn16f[kg * 128:(kg + 1) * 128, :])
            ring = WRing()
            blocks = {0: ring.load_block(W["w0T"], 0, 2048)}
            p.wait("tensor", "LD", lds)
            for og in range(FFC):
                if og + 1 < FFC:
                    blocks[og + 1] = ring.load_block(W["w0T"], (og + 1) * 128, 2048)
                chunks, po_t, first = blocks.pop(og)
                p.wait("tensor", "PO", po_t)
                for tg in range(NTG):
                    gi = og * 8 + tg
                    bank = MB[tg % 2]
                    p.wait("tensor", "AC", s6_ac.get(gi - 2, 0))
                    for kg in range(KC):
                        p.do("tensor", "matmul", out=bank[:, :], lhsT=chunks[kg],
                             rhs=XN16[kg][:, row_sl(tg)],
                             start=(kg == 0), stop=(kg == KC - 1),
                             inc=([("PE", 1)] if kg == KC - 1 else []))
                    pe = p.after("PE")
                    p.wait("scalar", "PE", pe)
                    p.wait("scalar", "PO", s6_po.get(gi - 1, 0))
                    p.do("scalar", "activation", out=F16T[:, :], in_=bank[:, :],
                         func=AF.Gelu, bias=b0c_t[:, og:og + 1], scale=1.0,
                         inc=[("AC", 1)])
                    s6_ac[gi] = p.after("AC")
                    p.xwait("gpsimd", "AC")
                    p.wait("gpsimd", "WR", s6_wr.get(gi - 2, 0))
                    p.do("gpsimd", "tensor_copy", out=EB[:, tg % 2, :],
                         in_=F16T[:, :], inc=[("PO", 1)])
                    s6_po[gi] = p.after("PO")
                    p.xwait("scalar", "PO")
                    dma_out(df[og * 128:(og + 1) * 128, row_sl(tg)],
                            EB[:, tg % 2, :])
                    s6_wr[gi] = p.after("WR")
                ring.mark(first, p.after("PE"))
                ring.mark(first + 1, p.after("PE"))
            p.barrier()

        s6_po = {}
        s6_wr = {}
        s6_ac = {}
        if N_STAGES > 5:
            s6()

        if N_STAGES > 6:
            ep_guard = {}
            s5_s7_common(df, w1R, s7_epilogue, hoist_rows=False)

        X1B = [RESf[:, kg * 512:(kg + 1) * 512] for kg in range(KC)]

        def s8():
            p.barrier()
            for tg in range(NTG):
                for kg in range(KC):
                    fi = tg * KC + kg
                    e0 = EV[:, 0, :]; e1 = EV[:, 1, :]; e2 = EV[:, 2, :]
                    p.wait("sync", "VE", s8_ve.get(fi - 1, 0))
                    l0 = dma_in(e0, x0f[kg * 128:(kg + 1) * 128, row_sl(tg)])
                    l1 = dma_in(e1, sf[kg * 128:(kg + 1) * 128, row_sl(tg)])
                    l2 = dma_in(e2, g0f[kg * 128:(kg + 1) * 128, row_sl(tg)])
                    p.wait("vector", "LD", l2)
                    p.wait("vector", "WR", s8_wrx.get((tg, kg - 2), 0))
                    p.do("vector", "tensor_tensor", out=e1, in0=e1, in1=e2,
                         op=ALU.mult)
                    p.do("vector", "tensor_tensor", out=X1B[kg], in0=e0, in1=e1,
                         op=ALU.add, inc=[("VE", 1)])
                    ve = p.after("VE")
                    s8_ve[fi] = ve
                    p.wait("scalar", "VE", ve)
                    dma_out(x1f[kg * 128:(kg + 1) * 128, row_sl(tg)], X1B[kg])
                    s8_wrx[(tg, kg)] = p.after("WR")
                    p.wait("scalar", "PS", s8_ps.get(kg - 2, 0))
                    p.do("scalar", "copy", out=EB[:, 0, :], in_=X1B[kg])
                    p.do("scalar", "square", out=EB[:, 1, :], in_=X1B[kg],
                         inc=[("AC", 1)])
                    p.xwait("tensor", "AC")
                    p.do("tensor", "matmul", out=SR[0][0:1, :], lhsT=ones_col_b,
                         rhs=EB[:, 0, :], start=(kg == 0), stop=(kg == KC - 1),
                         skip_group_check=True)
                    p.do("tensor", "matmul", out=SR[1][0:1, :], lhsT=ones_col_b,
                         rhs=EB[:, 1, :], start=(kg == 0), stop=(kg == KC - 1),
                         skip_group_check=True, inc=[("PS", 1)])
                    s8_ps[kg] = p.after("PS")
                p.xwait("vector", "PS")
                p.do("vector", "tensor_copy", out=ROW[0][0:1, :], in_=SR[0][0:1, :])
                p.do("vector", "tensor_copy", out=ROW[1][0:1, :], in_=SR[1][0:1, :],
                     inc=[("VE", 1)])
                p.xwait("tensor", "VE")
                Rv = lambda i: ROW[i][0:1, :]
                p.do("vector", "tensor_scalar", out=Rv(2), in0=Rv(0),
                     scalar1=1.0 / D, scalar2=None, op0=ALU.mult)
                p.do("vector", "tensor_scalar", out=Rv(3), in0=Rv(1),
                     scalar1=1.0 / D, scalar2=None, op0=ALU.mult)
                p.do("vector", "tensor_tensor", out=Rv(4), in0=Rv(2), in1=Rv(2),
                     op=ALU.mult)
                p.do("vector", "tensor_tensor", out=Rv(4), in0=Rv(3), in1=Rv(4),
                     op=ALU.subtract)
                rstd_row(4, 5, EPS_LN, 1.0)
                p.xwait("vector", "AC")
                p.do("vector", "tensor_tensor", out=Rv(1), in0=Rv(2), in1=Rv(5),
                     op=ALU.mult)
                p.do("vector", "tensor_scalar", out=Rv(1), in0=Rv(1), scalar1=-1.0,
                     scalar2=None, op0=ALU.mult)
                p.do("vector", "tensor_scalar", out=Rv(3), in0=Rv(4),
                     scalar1=EPS_LN, scalar2=None, op0=ALU.add)
                p.do("vector", "reciprocal", out=Rv(3), in_=Rv(3))
                p.do("vector", "tensor_tensor", out=Rv(3), in0=Rv(4), in1=Rv(3),
                     op=ALU.mult)
                rstd_row(3, 3, EPS_LN, 1.0)
                p.xwait("vector", "AC")
                p.do("vector", "tensor_tensor", out=Rv(0), in0=Rv(5), in1=Rv(3),
                     op=ALU.mult)
                p.do("vector", "tensor_tensor", out=Rv(2), in0=Rv(1), in1=Rv(3),
                     op=ALU.mult, inc=[("VE", 1)])
                vr = p.after("VE")
                p.wait("scalar", "VE", vr)
                dma_out(rowsd[6:7, row_sl(tg)], Rv(5))
                dma_out(rowsd[7:8, row_sl(tg)], Rv(1))
                dma_out(rowsd[8:9, row_sl(tg)], Rv(0))
                dma_out(rowsd[9:10, row_sl(tg)], Rv(2))
                bcast(0, 0, wait=("VE", vr))
                bcast(1, 2)
                for kg in range(KC):
                    p.wait("vector", "WR", s8_wrn.get((tg, kg - 2), 0))
                    p.do("vector", "tensor_tensor", out=EV[:, 3 + kg % 2, :],
                         in0=X1B[kg], in1=BC[:, 0, :], op=ALU.mult)
                    p.do("vector", "tensor_tensor", out=EV[:, 3 + kg % 2, :],
                         in0=EV[:, 3 + kg % 2, :], in1=BC[:, 1, :], op=ALU.add)
                    p.do("vector", "tensor_scalar", out=EB[:, 2 + kg % 2, :],
                         in0=EV[:, 3 + kg % 2, :],
                         scalar1=gammac_t[:, kg:kg + 1],
                         scalar2=betac_t[:, kg:kg + 1],
                         op0=ALU.mult, op1=ALU.add, inc=[("VE", 1)])
                    p.xwait("scalar", "VE")
                    dma_out(Xnf[kg * 128:(kg + 1) * 128, row_sl(tg)],
                            EB[:, 2 + kg % 2, :])
                    s8_wrn[(tg, kg)] = p.after("WR")
                p.xwait("vector", "WR")
            p.barrier()

        s8_wrx = {}
        s8_ps = {}
        s8_wrn = {}
        s8_ve = {}
        if N_STAGES > 7:
            s8()

        def s9_mm(b, og, bank):
            for kg in range(KC):
                p.do("tensor", "matmul", out=bank[:, :], lhsT=wres(kg, og),
                     rhs=AIN[b][kg], start=(kg == 0), stop=(kg == KC - 1),
                     inc=([("PE", 1)] if kg == KC - 1 else []))
            return p.after("PE")

        def s9_q_epilogue(b, ctg, state):
            for og in range(16):
                par = og % 2
                bank = MB[par]
                p.wait("tensor", "VE", state.get(("bankVE", par), 0))
                pe = s9_mm(b, og, bank)
                p.wait("vector", "PE", pe)
                p.wait("vector", "WR", state.get(("wr", par), 0))
                p.do("vector", "tensor_scalar", out=EB[:, par, :],
                     in0=bank[:, :], scalar1=binc_t[:, og:og + 1], scalar2=None,
                     op0=ALU.add, inc=[("VE", 1)])
                state[("bankVE", par)] = p.after("VE")
                p.xwait("scalar", "VE")
                dma_out(q1f[og * 128:(og + 1) * 128, row_sl(ctg)],
                        EB[:, par, :])
                state[("wr", par)] = p.after("WR")

        def s9_kv_epilogue(sec, b, ctg, state):
            out_dram = kpf if sec == 1 else vpf
            stat_row = 10 if sec == 1 else 11
            bias_off = sec * 16
            first_stat = True
            for c in range(4):
                ce, co = c, c + 4
                be, bo = MB[0], MB[1]
                p.wait("tensor", "VE", state.get(("bankVE", 0), 0))
                p.wait("tensor", "VE", state.get(("bankVE", 1), 0))
                pe_e = s9_mm(b, ce, be)
                pe_o = s9_mm(b, co, bo)
                p.wait("sync", "VE", state.get(("rotVE",), 0))
                lc = dma_in(RT[:, 0, :],
                            W["rotcos"][ce * 128:(ce + 1) * 128, row_sl(ctg)])
                ls = dma_in(RT[:, 1, :],
                            W["rotsin"][ce * 128:(ce + 1) * 128, row_sl(ctg)])
                p.wait("vector", "PE", pe_o)
                p.wait("vector", "LD", ls)
                ee = EV[:, 0, :]; eo = EV[:, 1, :]
                t0 = EV[:, 2, :]; t1 = EV[:, 3, :]
                cosT = RT[:, 0, :]; sinT = RT[:, 1, :]
                p.do("vector", "tensor_scalar", out=ee, in0=be[:, :],
                     scalar1=binc_t[:, bias_off + ce:bias_off + ce + 1],
                     scalar2=None, op0=ALU.add)
                p.do("vector", "tensor_scalar", out=eo, in0=bo[:, :],
                     scalar1=binc_t[:, bias_off + co:bias_off + co + 1],
                     scalar2=None, op0=ALU.add, inc=[("VE", 1)])
                state[("bankVE", 0)] = state[("bankVE", 1)] = p.after("VE")
                p.wait("vector", "WR", state.get(("tWR",), 0))
                p.wait("vector", "AC", state.get(("tAC",), 0))
                p.do("vector", "tensor_tensor", out=t0, in0=ee, in1=cosT,
                     op=ALU.mult)
                p.do("vector", "tensor_tensor", out=t1, in0=eo, in1=sinT,
                     op=ALU.mult)
                p.do("vector", "tensor_tensor", out=t0, in0=t0, in1=t1,
                     op=ALU.subtract)
                p.do("vector", "tensor_tensor", out=t1, in0=eo, in1=cosT,
                     op=ALU.mult)
                p.do("vector", "tensor_tensor", out=ee, in0=ee, in1=sinT,
                     op=ALU.mult)
                p.do("vector", "tensor_tensor", out=t1, in0=t1, in1=ee,
                     op=ALU.add, inc=[("VE", 1)])
                state[("rotVE",)] = p.after("VE")
                ve = p.after("VE")
                p.wait("scalar", "VE", ve)
                dma_out(out_dram[ce * 128:(ce + 1) * 128, row_sl(ctg)], t0)
                dma_out(out_dram[co * 128:(co + 1) * 128, row_sl(ctg)], t1)
                state[("tWR",)] = p.after("WR")
                p.wait("scalar", "PS", state.get(("ebPS",), 0))
                p.do("scalar", "square", out=EB[:, 0, :], in_=t0)
                p.do("scalar", "square", out=EB[:, 1, :], in_=t1, inc=[("AC", 1)])
                state[("tAC",)] = p.after("AC")
                p.xwait("tensor", "AC")
                p.do("tensor", "matmul", out=SR[0][0:1, :], lhsT=ones_col_b,
                     rhs=EB[:, 0, :], start=first_stat, stop=False,
                     skip_group_check=True)
                first_stat = False
                p.do("tensor", "matmul", out=SR[0][0:1, :], lhsT=ones_col_b,
                     rhs=EB[:, 1, :], start=False, stop=False,
                     skip_group_check=True, inc=[("PS", 1)])
                state[("ebPS",)] = p.after("PS")
            for og in range(8, 16):
                par = og % 2
                bank = MB[2 + par]
                p.wait("tensor", "VE", state.get(("pbankVE", par), 0))
                pe = s9_mm(b, og, bank)
                p.wait("vector", "PE", pe)
                p.wait("vector", "WR", state.get(("pwr", par), 0))
                p.wait("vector", "AC", state.get(("pAC", par), 0))
                e0 = EV[:, 4 + par, :]
                p.do("vector", "tensor_scalar", out=e0, in0=bank[:, :],
                     scalar1=binc_t[:, bias_off + og:bias_off + og + 1],
                     scalar2=None, op0=ALU.add, inc=[("VE", 1)])
                state[("pbankVE", par)] = p.after("VE")
                p.xwait("scalar", "VE")
                dma_out(out_dram[og * 128:(og + 1) * 128, row_sl(ctg)], e0)
                state[("pwr", par)] = p.after("WR")
                p.wait("scalar", "PS", state.get(("pebPS", par), 0))
                p.do("scalar", "square", out=EB[:, 2 + par, :], in_=e0,
                     inc=[("AC", 1)])
                state[("pAC", par)] = p.after("AC")
                p.xwait("tensor", "AC")
                p.do("tensor", "matmul", out=SR[0][0:1, :], lhsT=ones_col_b,
                     rhs=EB[:, 2 + par, :], start=False, stop=(og == 15),
                     skip_group_check=True, inc=[("PS", 1)])
                state[("pebPS", par)] = p.after("PS")
            p.xwait("vector", "PS")
            p.wait("vector", "PE", rowpe.get(2, 0))
            p.do("vector", "tensor_copy", out=ROW[2][0:1, :], in_=SR[0][0:1, :],
                 inc=[("VE", 1)])
            p.xwait("tensor", "VE")
            p.xwait("scalar", "VE")
            p.do("scalar", "activation", out=ROW[2][0:1, :], in_=ROW[2][0:1, :],
                 func=AF.Ln, bias=0.0, scale=1.0)
            p.do("scalar", "activation", out=ROW[2][0:1, :], in_=ROW[2][0:1, :],
                 func=AF.Exp, scale=-0.5, inc=[("AC", 1)])
            p.xwait("scalar", "AC")
            dma_out(rowsd[stat_row:stat_row + 1, row_sl(ctg)], ROW[2][0:1, :])

        def s9_section(sec):
            p.barrier()
            p.wait("sync", "POW", pow_mark[f"win{sec}"])
            ldw = load_sec_bf(winR, sec * 32)
            p.wait("tensor", "LD", ldw)
            xn_ld = {}
            blk_pe = {}
            state = {}
            for tg in range(NTG + 1):
                if tg < NTG:
                    b = tg % 2
                    if tg >= 2:
                        p.wait("sync", "PE", blk_pe[tg - 2])
                    t = None
                    for kg in range(KC):
                        t = dma_in(AIN[b][kg],
                                   Xnf[kg * 128:(kg + 1) * 128, row_sl(tg)])
                    xn_ld[tg] = t
                if tg == 0:
                    continue
                ctg = tg - 1
                b = ctg % 2
                p.wait("tensor", "LD", xn_ld[ctg])
                if sec == 0:
                    s9_q_epilogue(b, ctg, state)
                else:
                    s9_kv_epilogue(sec, b, ctg, state)
                blk_pe[ctg] = p.after("PE")
            p.barrier()

        if N_STAGES > 8:
            s9_section(0)
            s9_section(1)
            s9_section(2)
            for tg in range(NTG):
                load_row(0, 10, tg)
                l2 = load_row(1, 11, tg)
                p.wait("vector", "LD", l2)
                p.do("vector", "tensor_tensor", out=ROW[0][0:1, :],
                     in0=ROW[0][0:1, :], in1=ROW[1][0:1, :], op=ALU.mult,
                     inc=[("VE", 1)])
                p.xwait("scalar", "VE")
                dma_out(rowsd[10:11, row_sl(tg)], ROW[0][0:1, :])
                p.xwait("sync", "WR")
            p.barrier()

        def s10():
            for kg in range(KC):
                red = EV[:, 6, 0:8]
                for tg in range(NTG):
                    e0 = EV[:, tg % 2, :]
                    e1 = EV[:, 2 + tg % 2, :]
                    l0 = dma_in(e0, kpf[kg * 128:(kg + 1) * 128, row_sl(tg)])
                    l1 = dma_in(e1, vpf[kg * 128:(kg + 1) * 128, row_sl(tg)])
                    lr = load_row(4 + tg % 2, 10, tg)
                    bcast(4 + tg % 2, 4 + tg % 2, wait=("LD", lr))
                    p.wait("vector", "LD", l1)
                    p.do("vector", "tensor_tensor", out=e0, in0=e0, in1=e1,
                         op=ALU.mult)
                    p.do("vector", "tensor_tensor", out=e0, in0=e0,
                         in1=BC[:, 4 + tg % 2, :], op=ALU.mult)
                    p.do("vector", "reduce_sum", out=red[:, tg:tg + 1], in_=e0,
                         axis=AX.X, inc=[("VE", 1)])
                    p.xwait("sync", "VE")
                p.xwait("vector", "VE")
                p.do("vector", "reduce_sum", out=ACOL[:, kg:kg + 1], in_=red,
                     axis=AX.X, inc=[("VE", 1)])
            p.xwait("vector", "VE")
            p.do("vector", "tensor_scalar", out=ACOL[:, :], in0=ACOL[:, :],
                 scalar1=0.5, scalar2=0.5, op0=ALU.mult, op1=ALU.add,
                 inc=[("VE", 1)])
            p.barrier()

        if N_STAGES > 9:
            s10()

        def s11():
            p.barrier()
            p.wait("sync", "POW", pow_mark["bt"])
            load_sec_bf(wffnR, 0, res_off=0)
            load_sec_bf(aflatR, 0, res_off=32768, cols=256, npieces=16)
            load_sec_bf(btR, 0, res_off=36864, npieces=4)
            for kg in range(KC):
                dma_in(RESf[:, 20480 + kg * 64: 20480 + (kg + 1) * 64],
                       W["rw1T"][kg * 128:(kg + 1) * 128, :])
            ldw = dma_in(RESf[0:64, 21504:21536], W["rw2T"])
            p.wait("tensor", "LD", ldw)
            if "dbgW" in DEBUG:
                dbgW = dram("dbgW", [128, 49152], BF16)
                p.xwait("scalar", "PO")
                p.xwait("scalar", "LD")
                dma_out(dbgW[:, :], RES[:, 0:49152])
                p.barrier()

            waflat = lambda kg, og: RES[:, 32768 + kg * 256 + og * 128:
                                        32768 + kg * 256 + (og + 1) * 128]
            wbt = lambda kg, og: RES[:, 36864 + kg * 2048 + og * 128:
                                     36864 + kg * 2048 + (og + 1) * 128]
            wrw1 = lambda kg: RESf[:, 20480 + kg * 64: 20480 + (kg + 1) * 64]
            wrw2 = RESf[0:64, 21504:21536]
            XN1B = [[RES[:, 49152 + b * 8192 + kg * 512:
                         49152 + b * 8192 + (kg + 1) * 512] for kg in range(KC)]
                    for b in range(2)]
            WXA = [RES[:, 45056 + i * 512: 45056 + (i + 1) * 512] for i in range(2)]

            for tg in range(NTG):
                b = tg % 2
                load_row(0, 6, tg)
                l2 = load_row(1, 7, tg)
                bcast(0, 0, wait=("LD", l2))
                bcast(1, 1)
                if tg >= 2:
                    p.wait("sync", "PE", s11_blk.get(tg - 2, 0))
                for kg in range(KC):
                    fi = tg * KC + kg
                    e0 = EV[:, kg % 2, :]
                    ef = EV[:, 2 + kg % 2, :]
                    p.wait("sync", "VE", s11_ve.get(fi - 2, 0))
                    lx = dma_in(e0, x1f[kg * 128:(kg + 1) * 128, row_sl(tg)])
                    p.wait("vector", "LD", lx)
                    p.wait("vector", "PE", s11_rpe.get(fi - 2, 0))
                    p.wait("vector", "PO", s11_po.get(fi - 2, 0))
                    p.do("vector", "tensor_tensor", out=ef, in0=e0,
                         in1=BC[:, 0, :], op=ALU.mult)
                    p.do("vector", "tensor_tensor", out=ef, in0=ef,
                         in1=BC[:, 1, :], op=ALU.add, inc=[("VE", 1)])
                    ve = p.after("VE")
                    s11_ve[fi] = ve
                    p.wait("gpsimd", "VE", ve)
                    p.do("gpsimd", "tensor_copy", out=XN1B[b][kg], in_=ef,
                         inc=[("PO", 1)])
                    s11_po[fi] = p.after("PO")
                    p.wait("tensor", "VE", ve)
                    p.do("tensor", "matmul", out=XPS[0:64, :], lhsT=wrw1(kg),
                         rhs=ef, start=(kg == 0), stop=(kg == KC - 1),
                         skip_group_check=True, inc=[("PE", 1)])
                    s11_rpe[fi] = p.after("PE")
                pe = p.after("PE")
                p.wait("scalar", "PE", pe)
                p.do("scalar", "activation", out=LG[0:64, 0:512],
                     in_=XPS[0:64, :], func=AF.Silu, bias=rb1c_t[:, 0:1],
                     scale=1.0, inc=[("AC", 1)])
                p.xwait("tensor", "AC")
                p.do("tensor", "matmul", out=XPS[0:32, :], lhsT=wrw2,
                     rhs=LG[0:64, 0:512], start=True, stop=True, inc=[("PE", 1)])
                p.xwait("scalar", "PE")
                p.do("scalar", "copy", out=LG[0:32, 0:512], in_=XPS[0:32, :],
                     inc=[("AC", 1)])
                p.xwait("vector", "AC")
                p.do("vector", "tensor_scalar", out=LG[0:32, 0:512],
                     in0=LG[0:32, 0:512], scalar1=rb2c_t[:, 0:1], scalar2=None,
                     op0=ALU.add)
                p.do("vector", "tensor_scalar", out=LG[0:16, 0:512],
                     in0=LG[0:16, 0:512], scalar1=gatesc_t[:, 0:1], scalar2=None,
                     op0=ALU.add, inc=[("VE", 1)])
                p.xwait("tensor", "VE")
                for sub in range(4):
                    p.do("tensor", "transpose", out=TRB[:, 0:32],
                         in_=LG[0:32, sub * 128:(sub + 1) * 128],
                         identity=ident_f[0:32, 0:32], inc=[("PE", 1)])
                    p.xwait("vector", "PE")
                    p.do("vector", "tensor_copy", out=TK[:, sub * 56:sub * 56 + 16],
                         in_=TRB[:, 0:16], inc=[("VE", 1)])
                    p.xwait("tensor", "VE")
                if "dbgL" in DEBUG:
                    for sub in range(4):
                        dma_out(dbgL[tg * 512 + sub * 128:
                                     tg * 512 + (sub + 1) * 128, :],
                                TK[:, sub * 56:sub * 56 + 16])
                def tkv(sub, o, n=16):
                    return TK[:, sub * 56 + o: sub * 56 + o + n]
                for sub in range(4):
                    p.do("vector", "reduce_max", out=tkv(sub, 16, 1),
                         in_=tkv(sub, 0), axis=AX.X,
                         inc=([("VE", 1)] if sub == 3 else []))
                p.xwait("vector", "VE")
                for sub in range(4):
                    p.do("vector", "tensor_scalar", out=tkv(sub, 20),
                         in0=tkv(sub, 0), scalar1=tkv(sub, 16, 1), scalar2=None,
                         op0=ALU.is_equal,
                         inc=([("VE", 1)] if sub == 3 else []))
                p.xwait("vector", "VE")
                for sub in range(4):
                    p.do("vector", "scalar_tensor_tensor", out=tkv(sub, 36 - 36),
                         in0=tkv(sub, 20), scalar=-1e30, in1=tkv(sub, 0),
                         op0=ALU.mult, op1=ALU.add,
                         inc=([("VE", 1)] if sub == 3 else []))
                p.xwait("vector", "VE")
                for sub in range(4):
                    p.do("vector", "reduce_max", out=tkv(sub, 17, 1),
                         in_=tkv(sub, 0), axis=AX.X,
                         inc=([("VE", 1)] if sub == 3 else []))
                p.xwait("vector", "VE")
                for sub in range(4):
                    p.do("vector", "tensor_scalar", out=tkv(sub, 36),
                         in0=tkv(sub, 0), scalar1=tkv(sub, 17, 1), scalar2=None,
                         op0=ALU.is_equal,
                         inc=([("VE", 1)] if sub == 3 else []))
                for sub in range(4):
                    p.do("vector", "tensor_tensor", out=tkv(sub, 18, 1),
                         in0=tkv(sub, 17, 1), in1=tkv(sub, 16, 1),
                         op=ALU.subtract,
                         inc=([("VE", 1)] if sub == 3 else []))
                ve = p.after("VE")
                p.wait("scalar", "VE", ve)
                for sub in range(4):
                    p.do("scalar", "activation", out=tkv(sub, 19, 1),
                         in_=tkv(sub, 18, 1), func=AF.Exp,
                         inc=([("AC", 1)] if sub == 3 else []))
                p.xwait("vector", "AC")
                for sub in range(4):
                    p.do("vector", "tensor_scalar", out=tkv(sub, 18, 1),
                         in0=tkv(sub, 19, 1), scalar1=1.0, scalar2=None,
                         op0=ALU.add,
                         inc=([("VE", 1)] if sub == 3 else []))
                p.xwait("vector", "VE")
                for sub in range(4):
                    p.do("vector", "reciprocal", out=tkv(sub, 18, 1),
                         in_=tkv(sub, 18, 1),
                         inc=([("VE", 1)] if sub == 3 else []))
                p.xwait("vector", "VE")
                for sub in range(4):
                    p.do("vector", "tensor_tensor", out=tkv(sub, 19, 1),
                         in0=tkv(sub, 19, 1), in1=tkv(sub, 18, 1), op=ALU.mult,
                         inc=([("VE", 1)] if sub == 3 else []))
                p.xwait("vector", "VE")
                for sub in range(4):
                    p.do("vector", "tensor_scalar", out=tkv(sub, 20),
                         in0=tkv(sub, 20), scalar1=tkv(sub, 18, 1), scalar2=None,
                         op0=ALU.mult,
                         inc=([("VE", 1)] if sub == 3 else []))
                p.xwait("vector", "VE")
                for sub in range(4):
                    p.do("vector", "tensor_scalar", out=tkv(sub, 36),
                         in0=tkv(sub, 36), scalar1=tkv(sub, 19, 1), scalar2=None,
                         op0=ALU.mult,
                         inc=([("VE", 1)] if sub == 3 else []))
                p.xwait("vector", "VE")
                for sub in range(4):
                    p.do("vector", "tensor_tensor", out=tkv(sub, 20),
                         in0=tkv(sub, 20), in1=tkv(sub, 36), op=ALU.add,
                         inc=([("VE", 1)] if sub == 3 else []))
                cve = p.after("VE")
                if "dbgC" in DEBUG:
                    p.wait("scalar", "VE", cve)
                    for sub in range(4):
                        dma_out(dbgC[tg * 512 + sub * 128:
                                     tg * 512 + (sub + 1) * 128, :],
                                tkv(sub, 20))
                    p.xwait("vector", "WR")
                p.wait("tensor", "VE", cve)
                for sub in range(4):
                    p.do("tensor", "transpose", out=TRB[0:16, 0:128],
                         in_=tkv(sub, 20), identity=ident_f, inc=[("PE", 1)])
                    p.xwait("vector", "PE")
                    p.do("vector", "tensor_copy",
                         out=EB[0:16, 3, sub * 128:(sub + 1) * 128],
                         in_=TRB[0:16, 0:128], inc=[("VE", 1)])
                    p.xwait("tensor", "VE")
                for cc in range(2):
                    p.do("tensor", "matmul", out=XPS[:, :],
                         lhsT=(rep0_b if cc == 0 else rep1_b),
                         rhs=EB[0:16, 3, :], start=True, stop=True,
                         inc=[("PE", 1)])
                    p.xwait("vector", "PE")
                    p.do("vector", "tensor_copy", out=BC[:, 2 + cc, :],
                         in_=XPS[:, :], inc=[("VE", 1)])
                    p.xwait("tensor", "VE")
                for cc in range(2):
                    bank = MB[2 + cc]
                    for kg in range(KC):
                        p.do("tensor", "matmul", out=bank[:, :],
                             lhsT=waflat(kg, cc), rhs=XN1B[b][kg],
                             start=(kg == 0), stop=(kg == KC - 1),
                             inc=([("PE", 1)] if kg == KC - 1 else []))
                    p.xwait("vector", "PE")
                    p.do("vector", "tensor_tensor", out=WXA[cc], in0=bank[:, :],
                         in1=BC[:, 2 + cc, :], op=ALU.mult, inc=[("VE", 1)])
                p.xwait("tensor", "VE")
                for og in range(16):
                    gi2 = tg * 16 + og
                    bank = MB[og % 2]
                    p.wait("tensor", "AC", s11_ac.get(gi2 - 2, 0))
                    for kg in range(KC):
                        p.do("tensor", "matmul", out=bank[:, :], lhsT=wres(kg, og),
                             rhs=XN1B[b][kg], start=(kg == 0), stop=False,
                             skip_group_check=True)
                    for kg2 in range(2):
                        p.do("tensor", "matmul", out=bank[:, :],
                             lhsT=wbt(kg2, og), rhs=WXA[kg2], start=False,
                             stop=(kg2 == 1), skip_group_check=True,
                             inc=([("PE", 1)] if kg2 == 1 else []))
                    pe = p.after("PE")
                    p.wait("scalar", "PE", pe)
                    p.wait("scalar", "WR", s11_wr.get(gi2 - 2, 0))
                    p.do("scalar", "activation", out=EV[:, 6 + og % 2, :],
                         in_=bank[:, :], func=AF.Silu,
                         bias=bffnc_t[:, og:og + 1], scale=1.0,
                         inc=[("AC", 1)])
                    s11_ac[gi2] = p.after("AC")
                    p.xwait("scalar", "AC")
                    dma_out(g1f[og * 128:(og + 1) * 128, row_sl(tg)],
                            EV[:, 6 + og % 2, :])
                    s11_wr[gi2] = p.after("WR")
                s11_blk[tg] = p.after("PE")
            p.barrier()

        dbgL = dram("dbgL", [T, 16]) if "dbgL" in DEBUG else None
        dbgC = dram("dbgC", [T, 16]) if "dbgC" in DEBUG else None
        s11_blk = {}
        s11_ve = {}
        s11_rpe = {}
        s11_po = {}
        s11_ac = {}
        s11_wr = {}
        if N_STAGES > 10:
            s11()

        def s12():
            p.barrier()
            ldw = load_sec_bf(woutPR, 0)
            p.wait("tensor", "LD", ldw)
            QB = [[RES[:, 49152 + b * 8192 + kg * 512:
                       49152 + b * 8192 + (kg + 1) * 512] for kg in range(KC)]
                  for b in range(2)]
            OTM = [RESf[:, 16384 + s * 2048: 16384 + (s + 1) * 2048]
                   for s in range(4)]
            q_ld = {}
            q_sc = {}
            blk = {}
            for tg in range(NTG + 1):
                if tg < NTG:
                    b = tg % 2
                    if tg >= 2:
                        p.wait("sync", "PE", blk[tg - 2])
                    t = None
                    for kg in range(KC):
                        t = dma_in(QB[b][kg],
                                   q1f[kg * 128:(kg + 1) * 128, row_sl(tg)])
                    q_ld[tg] = t
                if tg == 0:
                    continue
                ctg = tg - 1
                b = ctg % 2
                load_row(0, 6, ctg)
                l2 = load_row(1, 7, ctg)
                bcast(0, 0, wait=("LD", l2))
                bcast(1, 1)
                p.wait("vector", "LD", q_ld[ctg])
                for kg in range(KC):
                    p.do("vector", "tensor_scalar", out=QB[b][kg], in0=QB[b][kg],
                         scalar1=ACOL[:, kg:kg + 1], scalar2=None, op0=ALU.mult,
                         inc=([("VE", 1)] if kg == KC - 1 else []))
                p.xwait("tensor", "VE")
                for og in range(16):
                    fi = ctg * 16 + og
                    bank = MB[og % 2]
                    p.wait("tensor", "VE", s12_ve.get(fi - 2, 0))
                    for kg in range(KC):
                        p.do("tensor", "matmul", out=bank[:, :], lhsT=wres(kg, og),
                             rhs=QB[b][kg], start=(kg == 0), stop=(kg == KC - 1),
                             inc=([("PE", 1)] if kg == KC - 1 else []))
                    pe = p.after("PE")
                    e0 = EV[:, og % 2, :]
                    e1 = EV[:, 2 + og % 2, :]
                    e2 = EV[:, 4 + og % 2, :]
                    p.wait("sync", "VE", s12_ve.get(fi - 2, 0))
                    lx = dma_in(e0, x1f[og * 128:(og + 1) * 128, row_sl(ctg)])
                    lg_ = dma_in(e1, g1f[og * 128:(og + 1) * 128, row_sl(ctg)])
                    p.wait("vector", "LD", lg_)
                    p.wait("vector", "PE", s12_tpe.get(fi - 2, 0))
                    p.do("vector", "tensor_tensor", out=e2, in0=e0,
                         in1=BC[:, 0, :], op=ALU.mult)
                    p.do("vector", "tensor_tensor", out=e2, in0=e2,
                         in1=BC[:, 1, :], op=ALU.add)
                    p.wait("vector", "PE", pe)
                    p.do("vector", "tensor_scalar", out=EV[:, 6 + og % 2, :],
                         in0=bank[:, :], scalar1=boutc_t[:, og:og + 1],
                         scalar2=None, op0=ALU.add)
                    p.do("vector", "tensor_tensor", out=e2, in0=EV[:, 6 + og % 2, :],
                         in1=e2, op=ALU.add)
                    p.do("vector", "tensor_tensor", out=e2, in0=e2, in1=e1,
                         op=ALU.mult)
                    p.do("vector", "tensor_tensor", out=e2, in0=e2, in1=e0,
                         op=ALU.add, inc=[("VE", 1)])
                    s12_ve[fi] = p.after("VE")
                    p.wait("tensor", "VE", s12_ve[fi])
                    for sub in range(4):
                        p.do("tensor", "transpose", out=TRB[:, :],
                             in_=e2[:, sub * 128:(sub + 1) * 128],
                             identity=ident_f, inc=[("PE", 1)])
                        p.xwait("vector", "PE")
                        p.do("vector", "tensor_copy",
                             out=OTM[sub][:, og * 128:(og + 1) * 128],
                             in_=TRB[:, :], inc=[("VE", 1)])
                        p.xwait("tensor", "VE")
                    s12_tpe[fi] = p.after("PE")
                blk[ctg] = p.after("PE")
                p.xwait("scalar", "VE")
                for sub in range(4):
                    dma_out(y_out[ctg * 512 + sub * 128: ctg * 512 + (sub + 1) * 128, :],
                            OTM[sub])
                p.wait("vector", "WR", p.after("WR"))
            p.barrier()
            p.xwait("sync", "WR")
            p.xwait("scalar", "WR")

        s12_ve = {}
        s12_tpe = {}
        if N_STAGES > 11:
            s12()

        sems = {}
        for s in p.semnames:
            sems[s] = ctx.enter_context(nc.semaphore(s))
        block = ctx.enter_context(nc.Block())

        def mk_body(engname):
            def body(eng):
                for fn in p.ops[engname]:
                    fn(eng, sems)
            return body
        block.sync(mk_body("sync"))
        block.tensor(mk_body("tensor"))
        block.vector(mk_body("vector"))
        block.scalar(mk_body("scalar"))
        block.gpsimd(mk_body("gpsimd"))
    return nc


def kernel(**inputs):
    x = np.ascontiguousarray(np.asarray(inputs["L0_attn_x"], np.float32))
    prep = host_prep(inputs)
    nc = build_program()
    in_maps = []
    for c in range(8):
        m = {"x": x[c]}
        m.update(prep)
        in_maps.append(m)
    res = run_bass_kernel_spmd(nc, in_maps, core_ids=list(range(8)))
    return np.stack([r["y"] for r in res.results], axis=0)
